# version 01 (final)
# baseline (speedup 1.0000x reference)
import contextlib
import math
import numpy as np
import concourse.bass as bass
import concourse.mybir as mybir
from concourse.bass_utils import run_bass_kernel_spmd

F32 = mybir.dt.float32
AF = mybir.ActivationFunctionType
OP = mybir.AluOpType
AX = mybir.AxisListType

D = 1024
S = 4096
H = 16
DH = 64
DHID = 4096
C = 512
CH = S // C
NT = S // 128
DK = D // 128
HT = DHID // 128


class Prog:
    def __init__(self):
        self.ops = []

    def do(self, eng, fn, inc=1):
        self.ops.append((eng, fn, inc))


def ln_fm(P, nc, src, dst, ps_b, ones, rows, mb, rb, sq, n_pt, inv_n, CL):
    oc = ones[:, 0:1]
    orow = ones[0:1, 0:128]
    m_row = rows[0:1, 0:CL]
    s_row = rows[0:1, 512:512 + CL]
    r_row = rows[0:1, 1024:1024 + CL]
    for t in range(n_pt):
        P.do('tensor', lambda e, t=t: e.matmul(ps_b[0:1, 0:CL], oc, src(t),
                                               start=(t == 0), stop=(t == n_pt - 1)))
    P.do('vector', lambda e: e.tensor_scalar_mul(m_row, ps_b[0:1, 0:CL], inv_n))
    P.do('tensor', lambda e: e.matmul(ps_b[:, 0:CL], orow, m_row, start=True, stop=True))
    P.do('vector', lambda e: e.tensor_copy(mb[:, 0:CL], ps_b[:, 0:CL]))
    for t in range(n_pt):
        P.do('vector', lambda e, t=t: e.tensor_tensor(src(t), src(t), mb[:, 0:CL], OP.subtract))
    for t in range(n_pt):
        P.do('scalar', lambda e, t=t: e.activation(sq[:, 0:CL], src(t), AF.Square))
        P.do('tensor', lambda e, t=t: e.matmul(ps_b[0:1, 0:CL], oc, sq[:, 0:CL],
                                               start=(t == 0), stop=(t == n_pt - 1)))
    P.do('scalar', lambda e: e.activation(s_row, ps_b[0:1, 0:CL], AF.Sqrt,
                                          scale=inv_n, bias=rows[0:1, 1536:1537]))
    P.do('vector', lambda e: e.reciprocal(r_row, s_row))
    P.do('tensor', lambda e: e.matmul(ps_b[:, 0:CL], orow, r_row, start=True, stop=True))
    P.do('vector', lambda e: e.tensor_copy(rb[:, 0:CL], ps_b[:, 0:CL]))
    for t in range(n_pt):
        P.do('vector', lambda e, t=t: e.tensor_tensor(dst(t), src(t), rb[:, 0:CL], OP.mult))


def build():
    nc = bass.Bass("TRN2", target_bir_lowering=False)
    x_in = nc.dram_tensor("x", [D, S], F32, kind="ExternalInput").ap()
    wq = nc.dram_tensor("wq", [D, 3 * D], F32, kind="ExternalInput").ap()
    wo_w = nc.dram_tensor("wo_w", [D, D], F32, kind="ExternalInput").ap()
    wexp = nc.dram_tensor("wexp", [D, DHID], F32, kind="ExternalInput").ap()
    wcon = nc.dram_tensor("wcon", [DHID, D], F32, kind="ExternalInput").ap()
    bq = nc.dram_tensor("bq", [128, 24], F32, kind="ExternalInput").ap()
    bo = nc.dram_tensor("bo", [128, 8], F32, kind="ExternalInput").ap()
    bexp = nc.dram_tensor("bexp", [128, 32], F32, kind="ExternalInput").ap()
    bcon = nc.dram_tensor("bcon", [128, 8], F32, kind="ExternalInput").ap()
    bdn = nc.dram_tensor("bdn", [128, 8], F32, kind="ExternalInput").ap()
    ub = nc.dram_tensor("ub", [128, 32], F32, kind="ExternalInput").ap()
    bwu = nc.dram_tensor("bwu", [128, 768], F32, kind="ExternalInput").ap()
    bwd = nc.dram_tensor("bwd", [128, 768], F32, kind="ExternalInput").ap()
    msk = nc.dram_tensor("msk", [128, 128], F32, kind="ExternalInput").ap()
    idm = nc.dram_tensor("idm", [128, 128], F32, kind="ExternalInput").ap()
    onesd = nc.dram_tensor("onesd", [128, 128], F32, kind="ExternalInput").ap()
    out_d = nc.dram_tensor("out", [D, S], F32, kind="ExternalOutput").ap()

    qkv_b = nc.dram_tensor("qkv_b", [3 * D, S], F32, kind="Internal").ap()
    attn_b = nc.dram_tensor("attn_b", [D, S], F32, kind="Internal").ap()
    x1_b = nc.dram_tensor("x1_b", [D, S], F32, kind="Internal").ap()
    ln_b = nc.dram_tensor("ln_b", [D, S], F32, kind="Internal").ap()
    h_b = nc.dram_tensor("h_b", [DHID, S], F32, kind="Internal").ap()
    f_b = nc.dram_tensor("f_b", [D, S], F32, kind="Internal").ap()

    def r3(dram, npt):
        return dram.rearrange("(t p) n -> p t n", p=128)

    with contextlib.ExitStack() as st:
        W = st.enter_context(nc.sbuf_tensor("W", [128, 16384], F32))
        S1 = st.enter_context(nc.sbuf_tensor("S1", [128, 8192], F32))
        S2 = st.enter_context(nc.sbuf_tensor("S2", [128, 8192], F32))
        S3 = st.enter_context(nc.sbuf_tensor("S3", [128, 4096], F32))
        VEC = st.enter_context(nc.sbuf_tensor("VEC", [128, 1700], F32))
        ONE = st.enter_context(nc.sbuf_tensor("ONE", [128, 128], F32))
        IDT = st.enter_context(nc.sbuf_tensor("IDT", [128, 128], F32))
        MSK = st.enter_context(nc.sbuf_tensor("MSK", [128, 128], F32))
        ROWS = st.enter_context(nc.sbuf_tensor("ROWS", [1, 1544], F32))
        COLS = st.enter_context(nc.sbuf_tensor("COLS", [128, 16], F32))
        ps_a = st.enter_context(nc.psum_tensor("ps_a", [128, 512], F32))
        ps_b = st.enter_context(nc.psum_tensor("ps_b", [128, 512], F32))
        ps_t = st.enter_context(nc.psum_tensor("ps_t", [128, 512], F32))
        ps_o = st.enter_context(nc.psum_tensor("ps_o", [128, 64], F32))
        ps_v = st.enter_context(nc.psum_tensor("ps_v", [128, 64], F32))
        g = st.enter_context(nc.semaphore("g"))
        block = st.enter_context(nc.Block())

        P = Prog()
        mb = S3[:, 512:1024]
        rb = S3[:, 1024:1536]
        sq = S3[:, 0:512]

        P.do('vector', lambda e: e.memset(ROWS[0:1, 1536:1537], 1e-5))
        P.do('sync', lambda e: e.dma_start(ONE[:, :], onesd), 16)
        P.do('sync', lambda e: e.dma_start(IDT[:, :], idm), 16)
        P.do('sync', lambda e: e.dma_start(MSK[:, :], msk), 16)
        P.do('sync', lambda e: e.dma_start(VEC[:, 0:24], bq), 16)
        P.do('sync', lambda e: e.dma_start(VEC[:, 24:32], bo), 16)
        P.do('sync', lambda e: e.dma_start(VEC[:, 32:64], bexp), 16)
        P.do('sync', lambda e: e.dma_start(VEC[:, 64:72], bcon), 16)
        P.do('sync', lambda e: e.dma_start(VEC[:, 72:80], bdn), 16)
        P.do('sync', lambda e: e.dma_start(VEC[:, 80:112], ub), 16)
        P.do('sync', lambda e: e.dma_start(VEC[:, 112:880], bwu), 16)
        P.do('sync', lambda e: e.dma_start(VEC[:, 880:1648], bwd), 16)

        x_r = r3(x_in, DK)
        ln_r = r3(ln_b, DK)
        x1_r = r3(x1_b, DK)
        at_r = r3(attn_b, DK)
        f_r = r3(f_b, DK)
        h_r = r3(h_b, HT)
        o_r = r3(out_d, DK)

        def s1v(c0, npt, CL):
            return S1[:, c0:c0 + npt * CL].rearrange("p (t n) -> p t n", n=CL)

        def s2v(c0, npt, CL):
            return S2[:, c0:c0 + npt * CL].rearrange("p (t n) -> p t n", n=CL)

        for ci in range(CH):
            c0 = ci * C
            P.do('sync', lambda e, c0=c0: e.dma_start(s1v(0, DK, C), x_r[:, :, c0:c0 + C]), 16)
            ln_fm(P, nc, lambda t: S1[:, t * C:(t + 1) * C], lambda t: S2[:, t * C:(t + 1) * C],
                  ps_b, ONE, ROWS, mb, rb, sq, DK, 1.0 / D, C)
            P.do('sync', lambda e, c0=c0: e.dma_start(ln_r[:, :, c0:c0 + C], s2v(0, DK, C)), 16)

        for ph in range(2):
            wv = W[:, 0:DK * 1536].rearrange("p (t n) -> p t n", n=1536)
            P.do('sync', lambda e, ph=ph, wv=wv: e.dma_start(
                wv, wq.rearrange("(t p) n -> p t n", p=128)[:, :, ph * 1536:(ph + 1) * 1536]), 16)
            for ci in range(CH):
                c0 = ci * C
                P.do('sync', lambda e, c0=c0: e.dma_start(s1v(0, DK, C), ln_r[:, :, c0:c0 + C]), 16)
                for ofl in range(12):
                    of = ph * 12 + ofl
                    for dk in range(DK):
                        P.do('tensor', lambda e, dk=dk, ofl=ofl: e.matmul(
                            ps_a[:, 0:C], W[:, dk * 1536 + ofl * 128: dk * 1536 + ofl * 128 + 128],
                            S1[:, dk * C:(dk + 1) * C], start=(dk == 0), stop=(dk == DK - 1)))
                    P.do('vector', lambda e, of=of: e.tensor_scalar(
                        S3[:, 0:C], ps_a[:, 0:C], VEC[:, of:of + 1], None, OP.add))
                    P.do('sync', lambda e, of=of, c0=c0: e.dma_start(
                        qkv_b[of * 128:(of + 1) * 128, c0:c0 + C], S3[:, 0:C]), 16)

        for h in range(H):
            qr = h * DH
            kr = D + h * DH
            vr = 2 * D + h * DH
            P.do('sync', lambda e, qr=qr: e.dma_start(S2[0:64, 0:4096], qkv_b[qr:qr + 64, :]), 16)
            P.do('sync', lambda e, kr=kr: e.dma_start(S2[0:64, 4096:8192], qkv_b[kr:kr + 64, :]), 16)
            P.do('sync', lambda e, vr=vr: e.dma_start(S1[0:64, 4096:8192], qkv_b[vr:vr + 64, :]), 16)
            for kt in range(NT):
                P.do('tensor', lambda e, kt=kt: e.transpose(
                    ps_v[:, 0:64], S1[0:64, 4096 + kt * 128:4096 + (kt + 1) * 128], IDT[0:64, 0:64]))
                P.do('vector', lambda e, kt=kt: e.tensor_copy(S3[:, kt * 64:(kt + 1) * 64], ps_v[:, 0:64]))
            for qb in range(NT):
                Wk = (qb + 1) * 128
                nch = (Wk + 511) // 512
                for kc in range(nch):
                    wc = min(512, Wk - kc * 512)
                    P.do('tensor', lambda e, qb=qb, kc=kc, wc=wc: e.matmul(
                        ps_a[:, 0:wc], S2[0:64, qb * 128:(qb + 1) * 128],
                        S2[0:64, 4096 + kc * 512:4096 + kc * 512 + wc], start=True, stop=True))
                    P.do('vector', lambda e, kc=kc, wc=wc: e.tensor_scalar(
                        S1[:, kc * 512:kc * 512 + wc], ps_a[:, 0:wc], 0.125, None, OP.mult))
                P.do('vector', lambda e, Wk=Wk: e.tensor_tensor(
                    S1[:, Wk - 128:Wk], S1[:, Wk - 128:Wk], MSK[:, :], OP.add))
                m = COLS[:, 0:1]; negm = COLS[:, 1:2]; sums = COLS[:, 2:3]; rr = COLS[:, 3:4]
                P.do('vector', lambda e, Wk=Wk: e.reduce_max(m, S1[:, 0:Wk], axis=AX.X))
                P.do('vector', lambda e: e.tensor_scalar_mul(negm, m, -1.0))
                P.do('scalar', lambda e, Wk=Wk: e.activation(
                    S1[:, 4096:4096 + Wk], S1[:, 0:Wk], AF.Exp, bias=negm, accum_out=sums))
                P.do('vector', lambda e: e.reciprocal(rr, sums))
                nkt = qb + 1
                for ktg in range((nkt + 3) // 4):
                    ng = min(4, nkt - ktg * 4)
                    for j in range(ng):
                        kt = ktg * 4 + j
                        P.do('tensor', lambda e, kt=kt, j=j: e.transpose(
                            ps_t[:, j * 128:(j + 1) * 128], S1[:, 4096 + kt * 128:4096 + (kt + 1) * 128],
                            IDT[:, :]))
                    P.do('vector', lambda e, ng=ng: e.tensor_copy(
                        S3[:, 2048:2048 + ng * 128], ps_t[:, 0:ng * 128]))
                    for j in range(ng):
                        kt = ktg * 4 + j
                        P.do('tensor', lambda e, kt=kt, j=j, nkt=nkt: e.matmul(
                            ps_o[:, 0:64], S3[:, 2048 + j * 128:2048 + (j + 1) * 128],
                            S3[:, kt * 64:(kt + 1) * 64], start=(kt == 0), stop=(kt == nkt - 1),
                            skip_group_check=True))
                P.do('vector', lambda e: e.tensor_scalar(
                    S3[:, 2560:2624], ps_o[:, 0:64], rr, None, OP.mult))
                P.do('tensor', lambda e: e.transpose(ps_t[0:64, 0:128], S3[:, 2560:2624], IDT[:, :]))
                P.do('vector', lambda e: e.tensor_copy(S3[0:64, 2624:2752], ps_t[0:64, 0:128]))
                P.do('sync', lambda e, h=h, qb=qb: e.dma_start(
                    attn_b[h * 64:(h + 1) * 64, qb * 128:(qb + 1) * 128], S3[0:64, 2624:2752]), 16)

        wv = W[:, 0:DK * 1024].rearrange("p (t n) -> p t n", n=1024)
        P.do('sync', lambda e, wv=wv: e.dma_start(wv, wo_w.rearrange("(t p) n -> p t n", p=128)), 16)
        for ci in range(CH):
            c0 = ci * C
            P.do('sync', lambda e, c0=c0: e.dma_start(s1v(0, DK, C), at_r[:, :, c0:c0 + C]), 16)
            for of in range(DK):
                for dk in range(DK):
                    P.do('tensor', lambda e, dk=dk, of=of: e.matmul(
                        ps_a[:, 0:C], W[:, dk * 1024 + of * 128: dk * 1024 + of * 128 + 128],
                        S1[:, dk * C:(dk + 1) * C], start=(dk == 0), stop=(dk == DK - 1)))
                P.do('vector', lambda e, of=of: e.tensor_scalar(
                    S2[:, of * C:(of + 1) * C], ps_a[:, 0:C], VEC[:, 24 + of:25 + of], None, OP.add))
            P.do('sync', lambda e, c0=c0: e.dma_start(s1v(4096, DK, C), x_r[:, :, c0:c0 + C]), 16)
            ln_fm(P, nc, lambda t: S2[:, t * C:(t + 1) * C], lambda t: S2[:, t * C:(t + 1) * C],
                  ps_b, ONE, ROWS, mb, rb, sq, DK, 1.0 / D, C)
            for t in range(DK):
                P.do('vector', lambda e, t=t: e.tensor_tensor(
                    S1[:, t * C:(t + 1) * C], S2[:, t * C:(t + 1) * C],
                    S1[:, 4096 + t * C:4096 + (t + 1) * C], OP.add))
            P.do('sync', lambda e, c0=c0: e.dma_start(x1_r[:, :, c0:c0 + C], s1v(0, DK, C)), 16)
            ln_fm(P, nc, lambda t: S1[:, t * C:(t + 1) * C], lambda t: S2[:, t * C:(t + 1) * C],
                  ps_b, ONE, ROWS, mb, rb, sq, DK, 1.0 / D, C)
            P.do('sync', lambda e, c0=c0: e.dma_start(ln_r[:, :, c0:c0 + C], s2v(0, DK, C)), 16)

        for ph in range(2):
            wv = W[:, 0:DK * 2048].rearrange("p (t n) -> p t n", n=2048)
            P.do('sync', lambda e, ph=ph, wv=wv: e.dma_start(
                wv, wexp.rearrange("(t p) n -> p t n", p=128)[:, :, ph * 2048:(ph + 1) * 2048]), 16)
            for ci in range(CH):
                c0 = ci * C
                P.do('sync', lambda e, c0=c0: e.dma_start(s1v(0, DK, C), ln_r[:, :, c0:c0 + C]), 16)
                for ofl in range(16):
                    of = ph * 16 + ofl
                    for dk in range(DK):
                        P.do('tensor', lambda e, dk=dk, ofl=ofl: e.matmul(
                            ps_a[:, 0:C], W[:, dk * 2048 + ofl * 128: dk * 2048 + ofl * 128 + 128],
                            S1[:, dk * C:(dk + 1) * C], start=(dk == 0), stop=(dk == DK - 1)))
                    P.do('vector', lambda e, of=of, ofl=ofl: e.tensor_scalar(
                        S2[:, ofl * C:(ofl + 1) * C], ps_a[:, 0:C], VEC[:, 32 + of:33 + of], None, OP.add))
                P.do('sync', lambda e, ph=ph, c0=c0: e.dma_start(
                    h_r[:, ph * 16:(ph + 1) * 16, c0:c0 + C], s2v(0, 16, C)), 16)

        def bf_stage(cur, alt, cur_regs, alt_regs, s, wbase):
            if s <= 4:
                mask = [(i ^ (1 << s)) for i in range(32)]
                for t in range(HT):
                    P.do('vector', lambda e, t=t, mask=mask: e.stream_shuffle(alt[t], cur[t], mask))
            elif s == 5:
                for (dreg, sreg) in zip(alt_regs, cur_regs):
                    for (a0, b0) in ((0, 32), (32, 0), (64, 96), (96, 64)):
                        P.do('sync', lambda e, dreg=dreg, sreg=sreg, a0=a0, b0=b0: e.dma_start(
                            dreg[a0:a0 + 32, :], sreg[b0:b0 + 32, :]), 16)
            elif s == 6:
                for t in range(HT):
                    P.do('vector', lambda e, t=t: e.tensor_copy(alt[t][0:64, :], cur[t][64:128, :]))
                    P.do('vector', lambda e, t=t: e.tensor_copy(alt[t][64:128, :], cur[t][0:64, :]))
            if s <= 6:
                for t in range(HT):
                    wo_v = VEC[:, wbase + (s * 2 + 1) * 32 + t: wbase + (s * 2 + 1) * 32 + t + 1]
                    P.do('vector', lambda e, t=t, wo_v=wo_v: e.tensor_scalar(
                        alt[t], alt[t], wo_v, None, OP.mult))
            else:
                mstep = 1 << (s - 7)
                for t in range(HT):
                    pt = t ^ mstep
                    wo_v = VEC[:, wbase + (s * 2 + 1) * 32 + t: wbase + (s * 2 + 1) * 32 + t + 1]
                    P.do('vector', lambda e, t=t, pt=pt, wo_v=wo_v: e.tensor_scalar(
                        alt[t], cur[pt], wo_v, None, OP.mult))
            for t in range(HT):
                wd_v = VEC[:, wbase + (s * 2) * 32 + t: wbase + (s * 2) * 32 + t + 1]
                P.do('vector', lambda e, t=t, wd_v=wd_v: e.scalar_tensor_tensor(
                    cur[t], cur[t], wd_v, alt[t], OP.mult, OP.add))

        for ci in range(CH):
            c0 = ci * C
            wv = W[:, 0:HT * C].rearrange("p (t n) -> p t n", n=C)
            P.do('sync', lambda e, c0=c0, wv=wv: e.dma_start(wv, h_r[:, :, c0:c0 + C]), 16)
            A = [W[:, t * C:(t + 1) * C] for t in range(HT)]
            Bl = [S1[:, t * C:(t + 1) * C] for t in range(16)] + \
                 [S2[:, t * C:(t + 1) * C] for t in range(16)]
            A_regs = [W[:, 0:8192], W[:, 8192:16384]]
            B_regs = [S1[:, 0:8192], S2[:, 0:8192]]
            for s in range(12):
                bf_stage(A, Bl, A_regs, B_regs, s, 112)
            for t in range(HT):
                P.do('scalar', lambda e, dst=Bl[t], src=A[t], t=t: e.activation(
                    dst, src, AF.Gelu, bias=VEC[:, 80 + t:81 + t]))
            A, Bl = Bl, A
            A_regs, B_regs = B_regs, A_regs
            for s in range(12):
                bf_stage(A, Bl, A_regs, B_regs, s, 880)
            P.do('sync', lambda e, c0=c0: e.dma_start(
                h_r[:, 0:16, c0:c0 + C], s1v(0, 16, C)), 16)
            P.do('sync', lambda e, c0=c0: e.dma_start(
                h_r[:, 16:32, c0:c0 + C], s2v(0, 16, C)), 16)

        for ph in range(2):
            wv = W[:, 0:HT * 512].rearrange("p (t n) -> p t n", n=512)
            P.do('sync', lambda e, ph=ph, wv=wv: e.dma_start(
                wv, wcon.rearrange("(t p) n -> p t n", p=128)[:, :, ph * 512:(ph + 1) * 512]), 16)
            for ci in range(CH):
                c0 = ci * C
                P.do('sync', lambda e, c0=c0: e.dma_start(s1v(0, 16, C), h_r[:, 0:16, c0:c0 + C]), 16)
                P.do('sync', lambda e, c0=c0: e.dma_start(s2v(0, 16, C), h_r[:, 16:32, c0:c0 + C]), 16)
                for ofl in range(4):
                    of = ph * 4 + ofl
                    for hk in range(HT):
                        src = S1[:, hk * C:(hk + 1) * C] if hk < 16 else S2[:, (hk - 16) * C:(hk - 15) * C]
                        P.do('tensor', lambda e, hk=hk, ofl=ofl, src=src: e.matmul(
                            ps_a[:, 0:C], W[:, hk * 512 + ofl * 128: hk * 512 + ofl * 128 + 128],
                            src, start=(hk == 0), stop=(hk == HT - 1)))
                    P.do('vector', lambda e, of=of: e.tensor_scalar(
                        S3[:, 0:C], ps_a[:, 0:C], VEC[:, 64 + of:65 + of],
                        VEC[:, 72 + of:73 + of], OP.add, OP.add))
                    P.do('sync', lambda e, of=of, c0=c0: e.dma_start(
                        f_b[of * 128:(of + 1) * 128, c0:c0 + C], S3[:, 0:C]), 16)

        for ci in range(CH):
            c0 = ci * C
            P.do('sync', lambda e, c0=c0: e.dma_start(s1v(0, DK, C), f_r[:, :, c0:c0 + C]), 16)
            P.do('sync', lambda e, c0=c0: e.dma_start(s1v(4096, DK, C), x1_r[:, :, c0:c0 + C]), 16)
            ln_fm(P, nc, lambda t: S1[:, t * C:(t + 1) * C], lambda t: S2[:, t * C:(t + 1) * C],
                  ps_b, ONE, ROWS, mb, rb, sq, DK, 1.0 / D, C)
            for t in range(DK):
                P.do('vector', lambda e, t=t: e.tensor_tensor(
                    S2[:, 4096 + t * C:4096 + (t + 1) * C], S2[:, t * C:(t + 1) * C],
                    S1[:, 4096 + t * C:4096 + (t + 1) * C], OP.add))
            P.do('sync', lambda e, c0=c0: e.dma_start(o_r[:, :, c0:c0 + C], s2v(4096, DK, C)), 16)

        waits = []
        cnt = 0
        for (eng, fn, inc) in P.ops:
            waits.append(cnt)
            cnt += inc
        total = cnt

        def emit(section_name, e):
            for i, (eng, fn, inc) in enumerate(P.ops):
                if eng != section_name:
                    continue
                if waits[i] > 0:
                    e.wait_ge(g, waits[i])
                inst = fn(e)
                inst.then_inc(g, inc)

        @block.sync
        def _(e):
            emit('sync', e)
            e.wait_ge(g, total)

        @block.tensor
        def _(e):
            emit('tensor', e)

        @block.vector
        def _(e):
            emit('vector', e)

        @block.scalar
        def _(e):
            emit('scalar', e)

    return nc


def kernel(L0_attn_x, L0_attn_w_qkv, L0_attn_b_qkv, L0_attn_w_o, L0_attn_b_o,
           L0_ffn_W_exp, L0_ffn_b_exp, L0_ffn_up_weights, L0_ffn_up_bias,
           L0_ffn_down_weights, L0_ffn_W_con, L0_ffn_b_con, L0_ffn_down_bias):
    f = np.float32

    def colvec(v, npt):
        return np.ascontiguousarray(np.asarray(v, f).reshape(npt, 128).T)

    def bfpack(Wst):
        Wst = np.asarray(Wst, f)
        idx = np.arange(4096)
        out = np.empty((12, 2, 4096), f)
        for s in range(12):
            bit = (idx >> s) & 1
            k = ((idx >> (s + 1)) << s) | (idx & ((1 << s) - 1))
            w = Wst[s]
            out[s, 0] = np.where(bit == 0, w[k, 0, 0], w[k, 1, 1])
            out[s, 1] = np.where(bit == 0, w[k, 0, 1], w[k, 1, 0])
        return np.ascontiguousarray(out.reshape(12, 2, 32, 128).transpose(3, 0, 1, 2).reshape(128, 768))

    ii, jj = np.indices((128, 128))
    mask = np.where(jj <= ii, 0.0, -1e30).astype(f)
    common = dict(
        wq=np.ascontiguousarray(np.asarray(L0_attn_w_qkv, f)),
        wo_w=np.ascontiguousarray(np.asarray(L0_attn_w_o, f)),
        wexp=np.ascontiguousarray(np.asarray(L0_ffn_W_exp, f)),
        wcon=np.ascontiguousarray(np.asarray(L0_ffn_W_con, f)),
        bq=colvec(L0_attn_b_qkv, 24), bo=colvec(L0_attn_b_o, 8),
        bexp=colvec(L0_ffn_b_exp, 32), bcon=colvec(L0_ffn_b_con, 8),
        bdn=colvec(L0_ffn_down_bias, 8), ub=colvec(L0_ffn_up_bias, 32),
        bwu=bfpack(L0_ffn_up_weights), bwd=bfpack(L0_ffn_down_weights),
        msk=mask, idm=np.eye(128, dtype=f), onesd=np.ones((128, 128), f),
    )
    x = np.asarray(L0_attn_x, f)
    in_maps = [dict(common, x=np.ascontiguousarray(x[b].T)) for b in range(8)]
    nc = build()
    res = run_bass_kernel_spmd(nc, in_maps, core_ids=list(range(8)))
    return np.stack([np.ascontiguousarray(r["out"].T) for r in res.results], axis=0)
